# version 06 (final)
# baseline (speedup 1.0000x reference)
import numpy as np
import concourse.bass as bass
import concourse.mybir as mybir
from concourse.bass_utils import run_bass_kernel_spmd

F32 = mybir.dt.float32
BF16 = mybir.dt.bfloat16
I32 = mybir.dt.int32
AF = mybir.ActivationFunctionType
OP = mybir.AluOpType

NCORES = 8
D = 4096
NT = 32
TOK = 2048
BTOK = 8192
BCH = 8
CH = 1024
NTT = 8
EPS_LN = 1e-5
EPS_KS = 1e-10

ENGS = ("tensor", "vector", "scalar", "sync", "gpsimd")


def _perm():
    r = np.arange(D)
    return 4 * (r % 1024) + (r // 1024)


class Sem:
    def __init__(self, handle, step):
        self.h = handle
        self.step = step
        self.n = 0

    def bump(self):
        self.n += self.step
        return self.n


class Prog:
    def __init__(self, sems):
        self.ops = {e: [] for e in ENGS}
        self.sems = sems
        self._lastw = {}

    def wait(self, eng, sem, val=None):
        if val is None:
            val = sem.n
        if val is None or val <= 0:
            return
        key = (eng, id(sem))
        if self._lastw.get(key, -1) >= val:
            return
        self._lastw[key] = val
        h = sem.h
        self.ops[eng].append(lambda e, h=h, v=val: e.wait_ge(h, v))

    def waitt(self, eng, mark):
        if not mark:
            return
        sem, val = mark
        self.wait(eng, sem, val)

    def op(self, eng, fn, inc=None):
        if inc is None:
            self.ops[eng].append(lambda e, fn=fn: fn(e))
            return None
        inc.bump()
        h, s = inc.h, inc.step
        self.ops[eng].append(lambda e, fn=fn, h=h, s=s: fn(e).then_inc(h, s))
        return inc.n

    def barrier(self):
        for eng in ENGS:
            for sem in self.sems:
                self.wait(eng, sem)

    def replay(self, blk, handles):
        for eng in ENGS:
            seq = self.ops[eng]

            def body(e, seq=seq):
                for fn in seq:
                    fn(e)

            getattr(blk, eng)(body)


def build_nc():
    from contextlib import ExitStack

    nc = bass.Bass("TRN2", target_bir_lowering=False, debug=False)
    ctx = ExitStack()

    xT = nc.dram_tensor("xT", [D, BTOK], F32, kind="ExternalInput").ap()
    wqT = nc.dram_tensor("wqT", [D, D], F32, kind="ExternalInput").ap()
    woT = nc.dram_tensor("woT", [D, D], F32, kind="ExternalInput").ap()
    wkvT = nc.dram_tensor("wkvT", [512, NT * 512], F32, kind="ExternalInput").ap()
    qw = nc.dram_tensor("qw", [128, NT * 3072], I32, kind="ExternalInput").ap()
    bq_t = nc.dram_tensor("bq_t", [128, 32], F32, kind="ExternalInput").ap()
    bo_t = nc.dram_tensor("bo_t", [128, 32], F32, kind="ExternalInput").ap()
    fsc_t = nc.dram_tensor("fsc_t", [128, 32], F32, kind="ExternalInput").ap()
    fbi_t = nc.dram_tensor("fbi_t", [128, 32], F32, kind="ExternalInput").ap()
    bkv_r = nc.dram_tensor("bkv_r", [1, 2048], F32, kind="ExternalInput").ap()
    zp_r = nc.dram_tensor("zp_r", [1, D], F32, kind="ExternalInput").ap()
    out = nc.dram_tensor("out", [D, TOK], F32, kind="ExternalOutput").ap()

    u_scr = nc.dram_tensor("u_scr", [NT * 128, D], BF16, kind="Internal").ap()
    x1_scr = nc.dram_tensor("x1_scr", [D, TOK], F32, kind="Internal").ap()
    lnx_scr = nc.dram_tensor("lnx_scr", [128, 2 * NT * CH], BF16, kind="Internal").ap()

    sb = lambda name, shape, dt: ctx.enter_context(nc.sbuf_tensor(name, shape, dt))
    act = sb("act", [128, NT * CH], BF16)
    bigb = sb("bigb", [128, NT * CH], BF16)
    wsu = sb("wsu", [128, 2 * D], BF16)
    qt = sb("qt", [128, 3072], I32)
    tmpi = sb("tmpi", [128, 1024], I32)
    xt = sb("xt", [128, 4096], F32)
    xb = sb("xb", [128, 1024], BF16)
    sqb = sb("sqb", [128, 1024], BF16)
    ptmp = sb("ptmp", [128, 1024], F32)
    phiq = sb("phiq", [128, 1024], BF16)
    pkv = sb("pkv", [128, 1024], BF16)
    kvbf = sb("kvbf", [128, 1024], BF16)
    kvacc = sb("kvacc", [128, 1024], F32)
    ksacc = sb("ksacc", [128, 8], F32)
    rcp = sb("rcp", [128, 16], F32)
    rowf = sb("rowf", [1, 1536], F32)
    ncs = sb("ncs", [1, 1024], BF16)
    bkvq = sb("bkvq", [1, 1024], BF16)
    zpb = sb("zpb", [1, 256], BF16)
    bqb = sb("bqb", [128, 32], F32)
    bob = sb("bob", [128, 32], F32)
    fscb = sb("fscb", [128, 32], F32)
    fbib = sb("fbib", [128, 32], F32)
    onec = sb("onec", [128, 1], BF16)
    onerb = sb("onerb", [1, 128], BF16)
    onerf = sb("onerf", [1, 128], F32)
    cst = sb("cst", [128, 8], F32)

    pb = [ctx.enter_context(nc.psum_tensor(f"pb{i}", [128, 512], F32)) for i in range(8)]

    nsem = 0

    def mksem(step):
        nonlocal nsem
        h = ctx.enter_context(nc.semaphore(f"s{nsem}"))
        nsem += 1
        return Sem(h, step)

    eT = mksem(1)
    eV = mksem(1)
    eS = mksem(1)
    sC = mksem(16)
    sQt = mksem(16)
    sUs = [mksem(16) for _ in range(2)]
    sB = mksem(16)
    sFu = [mksem(16) for _ in range(2)]
    sO = [mksem(16) for _ in range(2)]
    sLs = [mksem(16) for _ in range(2)]
    sLn = [mksem(16) for _ in range(2)]
    sWx = [mksem(16) for _ in range(2)]
    sX1 = [mksem(16) for _ in range(2)]
    sFx = [mksem(16) for _ in range(2)]
    sKv = [mksem(16) for _ in range(2)]
    sWq = [mksem(16) for _ in range(2)]
    sWo = [mksem(16) for _ in range(2)]
    sZp = [mksem(16) for _ in range(2)]

    allsems = [eT, eV, eS, sC, sQt, sB] + sUs + sFu + sO + sLs + sLn + sWx + sX1 + sFx + sKv + sWq + sWo + sZp
    P = Prog(allsems)

    P.op("vector", lambda e: e.memset(onec[:, :], 1.0), inc=eV)
    P.op("vector", lambda e: e.memset(onerb[:, :], 1.0), inc=eV)
    P.op("vector", lambda e: e.memset(onerf[:, :], 1.0), inc=eV)
    P.op("vector", lambda e: e.memset(cst[:, 0:1], EPS_LN), inc=eV)
    P.op("vector", lambda e: e.memset(cst[:, 1:2], -0.5), inc=eV)
    P.op("vector", lambda e: e.memset(cst[:, 2:3], -1.0), inc=eV)
    P.op("vector", lambda e: e.memset(kvacc[:, :], 0.0), inc=eV)
    P.op("vector", lambda e: e.memset(ksacc[:, :], 0.0), inc=eV)
    P.op("sync", lambda e: e.dma_start(out=bqb[:, :], in_=bq_t[:, :]), inc=sC)
    P.op("sync", lambda e: e.dma_start(out=bob[:, :], in_=bo_t[:, :]), inc=sC)
    P.op("sync", lambda e: e.dma_start(out=fscb[:, :], in_=fsc_t[:, :]), inc=sC)
    P.op("sync", lambda e: e.dma_start(out=fbib[:, :], in_=fbi_t[:, :]), inc=sC)
    P.barrier()

    dq = {"vread": [0] * NT, "store": [0] * NT, "load": [0] * NT, "next": 0}

    def deq_load(nt):
        if nt < 0 or nt >= NT:
            return
        if nt >= 1:
            P.wait("sync", eV, dq["vread"][nt - 1])
        src = qw[:, nt * 3072:(nt + 1) * 3072]
        dq["load"][nt] = (sQt, P.op("sync", lambda e, s=src: e.dma_start(out=qt[:, :], in_=s), inc=sQt))

    def deq_unpack(nt):
        ub = wsu[:, (nt % 2) * D:(nt % 2) * D + D]
        if nt >= 2:
            P.waitt("scalar", dq["store"][nt - 2])
        P.waitt("vector", dq["load"][nt])
        copy_marks = [0] * 8
        for k in range(8):
            j, sub = k // 2, k % 2
            srcoff = (0 if j <= 1 else 1024 * (j - 1)) + sub * 512
            ti = tmpi[:, (k % 2) * 512:(k % 2) * 512 + 512]
            if k >= 2:
                P.wait("vector", eS, copy_marks[k - 2])
            if j == 1:
                P.op("vector", lambda e, t=ti, s=srcoff: e.tensor_scalar(
                    t, qt[:, s:s + 512], 6, 63, OP.logical_shift_right, OP.bitwise_and), inc=eV)
            else:
                P.op("vector", lambda e, t=ti, s=srcoff: e.tensor_scalar(
                    t, qt[:, s:s + 512], 63, None, OP.bitwise_and), inc=eV)
            P.wait("scalar", eV, eV.n)
            P.op("scalar", lambda e, t=ti, u=ub, o=j * 1024 + sub * 512: e.copy(u[:, o:o + 512], t), inc=eS)
            copy_marks[k] = eS.n
        dq["vread"][nt] = eV.n
        P.wait("sync", eS, eS.n)
        dq["store"][nt] = (sUs[nt % 2], P.op("sync", lambda e, u=ub, nt=nt: e.dma_start(
            out=u_scr[nt * 128:(nt + 1) * 128, :], in_=u), inc=sUs[nt % 2]))

    def deq_step():
        nt = dq["next"]
        if nt >= NT:
            return
        dq["next"] += 1
        deq_unpack(nt)
        deq_load(nt + 1)

    deq_load(0)

    lnm = {}

    def ln_stats_pass(ch, tag, bS2):
        for rt in range(NT):
            b = rt % 2
            xtile = xt[:, b * 1024:b * 1024 + 1024]
            P.wait("scalar", eV, lnm.get(("vm", tag, rt - 2), eV.n if rt < 2 else 0))
            xl = (sLs[rt % 2], P.op("scalar", lambda e, d=xtile, r=rt: e.dma_start(
                out=d, in_=xT[r * 128:(r + 1) * 128, ch * CH:(ch + 1) * CH]), inc=sLs[rt % 2]))
            P.waitt("scalar", xl)
            P.wait("scalar", eT, lnm.get(("mm", tag, rt - 1), eT.n if rt < 1 else 0))
            P.op("scalar", lambda e, s=xtile: e.copy(xb[:, :], s), inc=eS)
            P.waitt("vector", xl)
            P.wait("vector", eT, lnm.get(("mm", tag, rt - 1), eT.n if rt < 1 else 0))
            P.op("vector", lambda e, s=xtile: e.tensor_mul(sqb[:, :], s, s), inc=eV)
            lnm[("vm", tag, rt)] = eV.n
            P.wait("tensor", eS, eS.n)
            P.wait("tensor", eV, eV.n)
            first = (rt == 0)
            for half in range(2):
                st = first
                P.op("tensor", lambda e, h=half, st=st: e.matmul(
                    pb[3][32 * h:32 * h + 1, :], onec[:, :], xb[:, h * 512:h * 512 + 512],
                    start=st, stop=(rt == NT - 1), skip_group_check=True))
                P.op("tensor", lambda e, h=half, st=st, b=bS2: e.matmul(
                    b[32 * h:32 * h + 1, :], onec[:, :], sqb[:, h * 512:h * 512 + 512],
                    start=st, stop=(rt == NT - 1), skip_group_check=True), inc=eT)
            lnm[("mm", tag, rt)] = eT.n
            deq_step()

    def ln_norm_pass(ch, dstbuf, tag, do_ncs, from_x1, bS2):
        src = x1_scr if from_x1 else xT
        for half in range(2):
            P.wait("vector", eT, lnm[("mm", tag, NT - 1)])
            if half == 1:
                P.wait("vector", eT, eT.n)
            m = rowf[:, 0:512]
            v = rowf[:, 512:1024]
            bb = rowf[:, 1024:1536]
            P.op("vector", lambda e, h=half: e.tensor_scalar_mul(m, pb[3][32 * h:32 * h + 1, :], 1.0 / D))
            P.op("vector", lambda e, h=half, b=bS2: e.tensor_scalar_mul(v, b[32 * h:32 * h + 1, :], 1.0 / D))
            P.op("vector", lambda e: e.tensor_mul(bb, m, m))
            P.op("vector", lambda e: e.tensor_sub(v, v, bb), inc=eV)
            P.wait("scalar", eV, eV.n)
            P.op("scalar", lambda e: e.activation(v, v, AF.Ln, bias=cst[0:1, 0:1]))
            P.op("scalar", lambda e: e.activation(v, v, AF.Exp, scale=cst[0:1, 1:2]), inc=eS)
            P.wait("vector", eS, eS.n)
            P.op("vector", lambda e: e.scalar_tensor_tensor(bb, m, -1.0, v, OP.mult, OP.mult), inc=eV)
            P.wait("tensor", eV, eV.n)
            P.wait("tensor", eS, eS.n)
            P.op("tensor", lambda e: e.matmul(pb[6][:, :], onerf[:, :], v,
                                              start=True, stop=True, skip_group_check=True))
            P.op("tensor", lambda e: e.matmul(pb[7][:, :], onerf[:, :], bb,
                                              start=True, stop=True, skip_group_check=True), inc=eT)
            bcast = eT.n
            for rt in range(NT):
                b = rt % 2
                xtile = xt[:, (2 + b) * 1024:(2 + b) * 1024 + 512]
                P.wait("scalar", eV, lnm.get(("nv", tag, half, rt - 2), eV.n if rt < 2 else 0))
                xl = (sLn[rt % 2], P.op("scalar", lambda e, d=xtile, r=rt, h=half, s=src: e.dma_start(
                    out=d, in_=s[r * 128:(r + 1) * 128, ch * CH + h * 512:ch * CH + h * 512 + 512]), inc=sLn[rt % 2]))
                P.waitt("vector", xl)
                P.wait("vector", eT, bcast)
                t1 = ptmp[:, b * 512:b * 512 + 512]
                dsl = dstbuf[:, rt * CH + half * 512: rt * CH + half * 512 + 512]
                P.op("vector", lambda e, x=xtile, t=t1: e.tensor_mul(t, x, pb[6][:, :]))
                P.op("vector", lambda e, t=t1, d=dsl: e.tensor_add(d, t, pb[7][:, :]), inc=eV)
                lnm[("nv", tag, half, rt)] = eV.n
                if do_ncs:
                    P.wait("tensor", eV, eV.n)
                    P.op("tensor", lambda e, d=dsl, h=half, st=(rt == 0): e.matmul(
                        pb[3][32 * h:32 * h + 1, :], onec[:, :], d,
                        start=st, stop=(rt == NT - 1), skip_group_check=True), inc=eT)
            if do_ncs:
                P.wait("vector", eT, eT.n)
                P.op("vector", lambda e, h=half: e.tensor_scalar_mul(
                    ncs[:, h * 512:h * 512 + 512], pb[3][32 * h:32 * h + 1, :], -1.0), inc=eV)

    kvm = {"wq": {}, "kvmm": {}, "ev": {}, "es": {}, "add": {}}

    def kv_pass(ch, lnbuf):
        P.wait("tensor", eV, eV.n)
        for q in range(4):
            qg = ch * 4 + q
            wbuf = bigb[:, (qg % 2) * 16384:(qg % 2) * 16384 + 16384]
            if qg >= 2:
                P.wait("gpsimd", eT, kvm["wq"][qg - 2])
            P.op("gpsimd", lambda e, w=wbuf, q=q: e.dma_start(
                out=w.rearrange("p (rt n) -> p rt n", n=512),
                in_=wkvT[q * 128:(q + 1) * 128, :].rearrange("p (rt n) -> p rt n", n=512)), inc=sKv[qg % 2])
            bq_ = bkvq[:, (qg % 2) * 512:(qg % 2) * 512 + 512]
            wload = (sKv[qg % 2], P.op("gpsimd", lambda e, b=bq_, q=q: e.dma_start(
                out=b, in_=bkv_r[0:1, q * 512:(q + 1) * 512]), inc=sKv[qg % 2]))
            P.waitt("tensor", wload)
            for tt in range(NTT):
                gi = qg * NTT + tt
                E = pb[4 + gi % 4]
                P.wait("tensor", eV, kvm["ev"].get(gi - 4, 0))
                P.wait("tensor", eS, kvm["es"].get(gi - 4, 0))
                for r in range(NT):
                    P.op("tensor", lambda e, E=E, r=r, tt=tt, w=wbuf: e.matmul(
                        E[:, :], lnbuf[:, r * CH + tt * 128: r * CH + tt * 128 + 128],
                        w[:, r * 512:(r + 1) * 512], start=(r == 0), stop=False,
                        skip_group_check=True))
                P.op("tensor", lambda e, E=E, b=bq_: e.matmul(
                    E[:, :], onerb[:, :], b, start=False, stop=True,
                    skip_group_check=True), inc=eT)
                mmv = eT.n
                st = tt % 2
                P.wait("vector", eT, mmv)
                P.wait("vector", eT, kvm["kvmm"].get(gi - 2, 0))
                P.op("vector", lambda e, E=E: e.tensor_scalar_min(ptmp[:, 0:256], E[:, 0:256], 0.0), inc=eV)
                P.wait("scalar", eV, eV.n)
                P.op("scalar", lambda e: e.activation(ptmp[:, 512:768], ptmp[:, 0:256], AF.Exp), inc=eS)
                P.wait("scalar", eT, kvm["kvmm"].get(gi - 2, 0))
                P.op("scalar", lambda e, E=E, s=st: e.copy(pkv[:, s * 512 + 256:s * 512 + 512], E[:, 256:512]), inc=eS)
                kvm["es"][gi] = eS.n
                P.wait("vector", eS, eS.n)
                P.op("vector", lambda e, E=E, s=st: e.scalar_tensor_tensor(
                    pkv[:, s * 512:s * 512 + 256], E[:, 0:256], 0.0, ptmp[:, 512:768],
                    OP.max, OP.add), inc=eV)
                kvm["ev"][gi] = eV.n
                P.wait("tensor", eV, eV.n)
                P.wait("tensor", eS, kvm["es"][gi])
                S = pb[gi % 2]
                P.wait("tensor", eV, kvm["add"].get(gi - 2, 0))
                for j in range(2):
                    P.op("tensor", lambda e, S=S, j=j, s=st: e.matmul(
                        S[:, j * 128:j * 128 + 128],
                        pkv[:, s * 512 + j * 128:s * 512 + j * 128 + 128],
                        pkv[:, s * 512 + 256 + j * 128:s * 512 + 256 + j * 128 + 128],
                        start=True, stop=True, skip_group_check=True))
                    P.op("tensor", lambda e, S=S, j=j, s=st: e.matmul(
                        S[:, 256 + j:256 + j + 1],
                        pkv[:, s * 512 + j * 128:s * 512 + j * 128 + 128],
                        onec[:, :], start=True, stop=True, skip_group_check=True), inc=eT)
                kvm["kvmm"][gi] = eT.n
                P.wait("vector", eT, kvm["kvmm"][gi])
                for j in range(2):
                    h = q * 2 + j
                    P.op("vector", lambda e, S=S, j=j, h=h: e.tensor_add(
                        kvacc[:, h * 128:h * 128 + 128], kvacc[:, h * 128:h * 128 + 128],
                        S[:, j * 128:j * 128 + 128]))
                    P.op("vector", lambda e, S=S, j=j, h=h: e.tensor_add(
                        ksacc[:, h:h + 1], ksacc[:, h:h + 1], S[:, 256 + j:256 + j + 1]), inc=eV)
                kvm["add"][gi] = eV.n
                deq_step()
            kvm["wq"][qg] = eT.n

    bounce = {}
    for ch in range(BCH):
        ln_stats_pass(ch, f"ln1:{ch}", pb[4])
        if ch >= 1 and (ch - 1) in bounce:
            P.waitt("vector", bounce[ch - 1])
        ln_norm_pass(ch, act, f"ln1:{ch}", False, False, pb[4])
        kv_pass(ch, act)
        if ch < 2:
            P.wait("sync", eT, eT.n)
            bounce[ch] = (sB, P.op("sync", lambda e, ch=ch: e.dma_start(
                out=lnx_scr[:, ch * NT * CH:(ch + 1) * NT * CH], in_=act[:, :]), inc=sB))
    while dq["next"] < NT:
        deq_step()
    P.barrier()

    P.op("vector", lambda e: e.tensor_copy(kvbf[:, :], kvacc[:, :]))
    P.op("vector", lambda e: e.tensor_scalar_add(rcp[:, 8:16], ksacc[:, 0:8], EPS_KS), inc=eV)
    P.wait("scalar", eV, eV.n)
    P.op("scalar", lambda e: e.activation(rcp[:, 0:8], rcp[:, 8:16], AF.Ln))
    P.op("scalar", lambda e: e.activation(rcp[:, 0:8], rcp[:, 0:8], AF.Exp, scale=cst[:, 2:3]), inc=eS)
    P.op("vector", lambda e: e.engine_nop(), inc=eV)
    kv_evac = eV.n
    recip_done = eV.n
    reloadB = (sB, P.op("sync", lambda e: e.dma_start(
        out=act[:, :], in_=lnx_scr[:, NT * CH:2 * NT * CH]), inc=sB))

    def qattn_pass(ch, lnbuf, attnbuf):
        mk = {}
        for nt in range(NT):
            wst = wsu[:, (nt % 2) * D:(nt % 2) * D + D]
            P.wait("gpsimd", eT, mk.get(("q", nt - 2), 0))
            wload = (sWq[nt % 2], P.op("gpsimd", lambda e, w=wst, nt=nt: e.dma_start(
                out=w.rearrange("p (rt n) -> p rt n", n=128),
                in_=wqT[nt * 128:(nt + 1) * 128, :].rearrange("p (rt n) -> p rt n", n=128)), inc=sWq[nt % 2]))
            E0 = pb[4] if nt % 2 == 0 else pb[0]
            E1 = pb[5] if nt % 2 == 0 else pb[1]
            A0 = pb[6] if nt % 2 == 0 else pb[2]
            A1 = pb[7] if nt % 2 == 0 else pb[3]
            P.waitt("tensor", wload)
            P.wait("tensor", eV, mk.get(("phi", nt - 2), kv_evac if nt < 2 else 0))
            for r in range(NT):
                P.op("tensor", lambda e, E=E0, r=r, w=wst: e.matmul(
                    E[:, :], w[:, r * 128:(r + 1) * 128], lnbuf[:, r * CH:r * CH + 512],
                    start=(r == 0), stop=(r == NT - 1), skip_group_check=True))
                P.op("tensor", lambda e, E=E1, r=r, w=wst: e.matmul(
                    E[:, :], w[:, r * 128:(r + 1) * 128], lnbuf[:, r * CH + 512:r * CH + 1024],
                    start=(r == 0), stop=(r == NT - 1), skip_group_check=True), inc=eT)
            mk[("q", nt)] = eT.n
            for half, E in ((0, E0), (1, E1)):
                P.wait("vector", eT, mk[("q", nt)])
                P.op("vector", lambda e, E=E, nt=nt: e.tensor_scalar(
                    ptmp[:, 0:512], E[:, :], bqb[:, nt:nt + 1], 0.0, OP.add, OP.min), inc=eV)
                P.wait("scalar", eV, eV.n)
                P.op("scalar", lambda e: e.activation(ptmp[:, 512:1024], ptmp[:, 0:512], AF.Exp), inc=eS)
                P.op("vector", lambda e, E=E, nt=nt, h=half: e.tensor_scalar(
                    phiq[:, h * 512:h * 512 + 512], E[:, :], bqb[:, nt:nt + 1], 0.0, OP.add, OP.max))
                P.wait("vector", eS, eS.n)
                P.op("vector", lambda e, h=half: e.tensor_add(
                    phiq[:, h * 512:h * 512 + 512], phiq[:, h * 512:h * 512 + 512],
                    ptmp[:, 512:1024]), inc=eV)
            mk[("phi", nt)] = eV.n
            kvh = nt // 4
            P.wait("tensor", eV, mk[("phi", nt)])
            P.wait("tensor", eV, recip_done)
            P.wait("tensor", eS, mk.get(("aev", nt - 2), 0))
            P.op("tensor", lambda e, A=A0, kvh=kvh: e.matmul(
                A[:, :], kvbf[:, kvh * 128:kvh * 128 + 128], phiq[:, 0:512],
                start=True, stop=True, skip_group_check=True))
            P.op("tensor", lambda e, A=A1, kvh=kvh: e.matmul(
                A[:, :], kvbf[:, kvh * 128:kvh * 128 + 128], phiq[:, 512:1024],
                start=True, stop=True, skip_group_check=True), inc=eT)
            mk[("attn", nt)] = eT.n
            P.wait("scalar", eT, mk[("attn", nt)])
            P.op("scalar", lambda e, A=A0, nt=nt, kvh=kvh: e.activation(
                attnbuf[:, nt * CH:nt * CH + 512], A[:, :], AF.Copy, scale=rcp[:, kvh:kvh + 1]))
            P.op("scalar", lambda e, A=A1, nt=nt, kvh=kvh: e.activation(
                attnbuf[:, nt * CH + 512:nt * CH + 1024], A[:, :], AF.Copy,
                scale=rcp[:, kvh:kvh + 1]), inc=eS)
            mk[("aev", nt)] = eS.n
        return mk

    def wo_pass(ch, attnbuf, tag):
        mk = {}
        ev0 = eV.n
        es0 = eS.n
        et0 = eT.n
        for rt in range(NT):
            wst = wsu[:, (rt % 2) * D:(rt % 2) * D + D]
            P.wait("gpsimd", eT, mk.get(("wo", rt - 2), et0 if rt < 2 else 0))
            wload = (sWo[rt % 2], P.op("gpsimd", lambda e, w=wst, rt=rt: e.dma_start(
                out=w.rearrange("p (rt2 n) -> p rt2 n", n=128),
                in_=woT[rt * 128:(rt + 1) * 128, :].rearrange("p (rt2 n) -> p rt2 n", n=128)), inc=sWo[rt % 2]))
            b = rt % 2
            xtile = xt[:, b * 1024:b * 1024 + 1024]
            P.wait("scalar", eV, mk.get(("x1", rt - 2), 0))
            xload = (sWx[rt % 2], P.op("scalar", lambda e, d=xtile, rt=rt: e.dma_start(
                out=d, in_=xT[rt * 128:(rt + 1) * 128, ch * CH:(ch + 1) * CH]), inc=sWx[rt % 2]))
            E0 = pb[4] if rt % 2 == 0 else pb[6]
            E1 = pb[5] if rt % 2 == 0 else pb[7]
            P.waitt("tensor", wload)
            P.wait("tensor", eV, mk.get(("x1", rt - 2), ev0 if rt < 2 else 0))
            P.wait("tensor", eS, es0 if rt < 2 else 0)
            for at in range(NT):
                P.op("tensor", lambda e, E=E0, a=at, w=wst, ab=attnbuf: e.matmul(
                    E[:, :], w[:, a * 128:(a + 1) * 128], ab[:, a * CH:a * CH + 512],
                    start=(a == 0), stop=(a == NT - 1), skip_group_check=True))
                P.op("tensor", lambda e, E=E1, a=at, w=wst, ab=attnbuf: e.matmul(
                    E[:, :], w[:, a * 128:(a + 1) * 128], ab[:, a * CH + 512:a * CH + 1024],
                    start=(a == 0), stop=(a == NT - 1), skip_group_check=True), inc=eT)
            mk[("wo", rt)] = eT.n
            x1t = xt[:, (2 + b) * 1024:(2 + b) * 1024 + 1024]
            P.wait("vector", eT, mk[("wo", rt)])
            P.waitt("vector", xload)
            P.waitt("vector", mk.get(("x1dma", rt - 2)))
            P.wait("vector", eS, lnm.get(("sc", tag, rt - 2), 0))
            P.op("vector", lambda e, E=E0, rt=rt, x=xtile, o=x1t: e.scalar_tensor_tensor(
                o[:, 0:512], E[:, :], bob[:, rt:rt + 1], x[:, 0:512], OP.add, OP.add))
            P.op("vector", lambda e, E=E1, rt=rt, x=xtile, o=x1t: e.scalar_tensor_tensor(
                o[:, 512:1024], E[:, :], bob[:, rt:rt + 1], x[:, 512:1024], OP.add, OP.add), inc=eV)
            mk[("x1", rt)] = eV.n
            P.wait("scalar", eV, mk[("x1", rt)])
            mk[("x1dma", rt)] = (sX1[rt % 2], P.op("scalar", lambda e, o=x1t, rt=rt: e.dma_start(
                out=x1_scr[rt * 128:(rt + 1) * 128, ch * CH:(ch + 1) * CH], in_=o), inc=sX1[rt % 2]))
            P.wait("scalar", eT, lnm.get(("mm", tag, rt - 1), 0))
            P.op("scalar", lambda e, o=x1t: e.copy(xb[:, :], o), inc=eS)
            lnm[("sc", tag, rt)] = eS.n
            P.wait("vector", eT, lnm.get(("mm", tag, rt - 1), 0))
            P.op("vector", lambda e, o=x1t: e.tensor_mul(sqb[:, :], o, o), inc=eV)
            P.wait("tensor", eS, eS.n)
            P.wait("tensor", eV, eV.n)
            for half in range(2):
                st = (rt == 0)
                P.op("tensor", lambda e, h=half, st=st: e.matmul(
                    pb[3][32 * h:32 * h + 1, :], onec[:, :], xb[:, h * 512:h * 512 + 512],
                    start=st, stop=(rt == NT - 1), skip_group_check=True))
                P.op("tensor", lambda e, h=half, st=st: e.matmul(
                    pb[2][32 * h:32 * h + 1, :], onec[:, :], sqb[:, h * 512:h * 512 + 512],
                    start=st, stop=(rt == NT - 1), skip_group_check=True), inc=eT)
            lnm[("mm", tag, rt)] = eT.n
        return mk

    def ffn_pass(ch, lnbuf):
        mk = {}
        for nt in range(NT):
            wst = wsu[:, (nt % 2) * D:(nt % 2) * D + D]
            P.wait("sync", eT, mk.get(("mm", nt - 2), 0))
            uload = (sFu[nt % 2], P.op("sync", lambda e, w=wst, nt=nt: e.dma_start(
                out=w, in_=u_scr[nt * 128:(nt + 1) * 128, :]), inc=sFu[nt % 2]))
            zpp = zpb[:, (nt % 2) * 128:(nt % 2) * 128 + 128]
            P.wait("gpsimd", eT, mk.get(("mm", nt - 2), 0))
            zpl = (sZp[nt % 2], P.op("gpsimd", lambda e, z=zpp, nt=nt: e.dma_start(
                out=z, in_=zp_r[0:1, nt * 128:(nt + 1) * 128]), inc=sZp[nt % 2]))
            b = nt % 2
            x1t = xt[:, b * 1024:b * 1024 + 1024]
            P.wait("scalar", eV, mk.get(("out", nt - 2), 0))
            x1l = (sFx[nt % 2], P.op("scalar", lambda e, d=x1t, nt=nt: e.dma_start(
                out=d, in_=x1_scr[nt * 128:(nt + 1) * 128, ch * CH:(ch + 1) * CH]), inc=sFx[nt % 2]))
            E0 = pb[4] if nt % 2 == 0 else pb[6]
            E1 = pb[5] if nt % 2 == 0 else pb[7]
            P.waitt("tensor", uload)
            P.waitt("tensor", zpl)
            P.wait("tensor", eS, mk.get(("sev", nt - 2), 0))
            for r in range(NT):
                P.op("tensor", lambda e, E=E0, r=r, w=wst: e.matmul(
                    E[:, :], w[:, r * 128:(r + 1) * 128], lnbuf[:, r * CH:r * CH + 512],
                    start=(r == 0), stop=False, skip_group_check=True))
                P.op("tensor", lambda e, E=E1, r=r, w=wst: e.matmul(
                    E[:, :], w[:, r * 128:(r + 1) * 128], lnbuf[:, r * CH + 512:r * CH + 1024],
                    start=(r == 0), stop=False, skip_group_check=True))
            P.op("tensor", lambda e, E=E0, z=zpp: e.matmul(
                E[:, :], z, ncs[:, 0:512], start=False, stop=True, skip_group_check=True))
            P.op("tensor", lambda e, E=E1, z=zpp: e.matmul(
                E[:, :], z, ncs[:, 512:1024], start=False, stop=True, skip_group_check=True), inc=eT)
            mk[("mm", nt)] = eT.n
            outt = xt[:, (2 + b) * 1024:(2 + b) * 1024 + 1024]
            P.wait("scalar", eT, mk[("mm", nt)])
            P.wait("scalar", eV, mk.get(("out", nt - 1), 0))
            P.op("scalar", lambda e, E=E0, nt=nt: e.activation(
                ptmp[:, 0:512], E[:, :], AF.Copy, scale=fscb[:, nt:nt + 1]))
            P.op("scalar", lambda e, E=E1, nt=nt: e.activation(
                ptmp[:, 512:1024], E[:, :], AF.Copy, scale=fscb[:, nt:nt + 1]), inc=eS)
            mk[("sev", nt)] = eS.n
            P.wait("vector", eS, mk[("sev", nt)])
            P.waitt("vector", x1l)
            P.waitt("vector", mk.get(("odma", nt - 2)))
            P.op("vector", lambda e, o=outt, x=x1t, nt=nt: e.scalar_tensor_tensor(
                o[:, 0:512], ptmp[:, 0:512], fbib[:, nt:nt + 1], x[:, 0:512], OP.add, OP.add))
            P.op("vector", lambda e, o=outt, x=x1t, nt=nt: e.scalar_tensor_tensor(
                o[:, 512:1024], ptmp[:, 512:1024], fbib[:, nt:nt + 1], x[:, 512:1024],
                OP.add, OP.add), inc=eV)
            mk[("out", nt)] = eV.n
            P.wait("sync", eV, mk[("out", nt)])
            mk[("odma", nt)] = (sO[nt % 2], P.op("sync", lambda e, o=outt, nt=nt: e.dma_start(
                out=out[nt * 128:(nt + 1) * 128, ch * CH:(ch + 1) * CH], in_=o), inc=sO[nt % 2]))
        return mk

    P.waitt("tensor", reloadB)
    qattn_pass(1, act, bigb)
    wo_pass(1, bigb, "ln2B")
    P.barrier()
    ln_norm_pass(1, act, "ln2B", True, True, pb[2])
    P.barrier()
    reloadA = (sB, P.op("sync", lambda e: e.dma_start(
        out=bigb[:, :], in_=lnx_scr[:, 0:NT * CH]), inc=sB))
    ffn_pass(1, act)
    P.barrier()
    P.waitt("tensor", reloadA)
    qattn_pass(0, bigb, act)
    wo_pass(0, act, "ln2A")
    P.barrier()
    ln_norm_pass(0, bigb, "ln2A", True, True, pb[2])
    P.barrier()
    ffn_pass(0, bigb)
    P.barrier()

    with nc.Block() as blk:
        handles = {
            "tensor": nc.tensor, "vector": nc.vector, "scalar": nc.scalar,
            "sync": nc.sync, "gpsimd": nc.gpsimd,
        }
        P.replay(blk, handles)

    nc._kernel_ctx = ctx
    return nc


def kernel(L0_attn_hidden_states, L0_attn_Wq, L0_attn_bq, L0_attn_Wk, L0_attn_bk,
           L0_attn_Wv, L0_attn_bv, L0_attn_Wo, L0_attn_bo, L0_ffn_qweight,
           L0_ffn_weight_scale, L0_ffn_weight_zero_point, L0_ffn_bias):
    f = _perm()
    x = np.asarray(L0_attn_hidden_states, dtype=np.float32).reshape(NCORES * TOK, D)
    Wq = np.asarray(L0_attn_Wq, dtype=np.float32)
    Wk = np.asarray(L0_attn_Wk, dtype=np.float32)
    Wv = np.asarray(L0_attn_Wv, dtype=np.float32)
    Wo = np.asarray(L0_attn_Wo, dtype=np.float32)
    bq = np.asarray(L0_attn_bq, dtype=np.float32)
    bk = np.asarray(L0_attn_bk, dtype=np.float32)
    bv = np.asarray(L0_attn_bv, dtype=np.float32)
    bo = np.asarray(L0_attn_bo, dtype=np.float32)
    qweight = np.asarray(L0_ffn_qweight, dtype=np.int32)
    fscale = np.asarray(L0_ffn_weight_scale, dtype=np.float32).reshape(D)
    fzp = np.asarray(L0_ffn_weight_zero_point, dtype=np.float32).reshape(D)
    fbias = np.asarray(L0_ffn_bias, dtype=np.float32)

    C = np.ascontiguousarray

    wqT = C(Wq[:, f].T.reshape(NT, 128, NT, 128).transpose(2, 1, 0, 3).reshape(D, D))
    woT = C(Wo[f, :].T.reshape(NT, 128, NT, 128).transpose(2, 1, 0, 3).reshape(D, D))
    WkT = Wk[:, f].T
    WvT = Wv[:, f].T
    kv_cols = np.empty((D, 2048), np.float32)
    bkv = np.empty(2048, np.float32)
    for q in range(4):
        for j in range(2):
            h = 2 * q + j
            kv_cols[:, q * 512 + j * 128:q * 512 + j * 128 + 128] = WkT[:, h * 128:(h + 1) * 128]
            kv_cols[:, q * 512 + 256 + j * 128:q * 512 + 256 + j * 128 + 128] = WvT[:, h * 128:(h + 1) * 128]
            bkv[q * 512 + j * 128:q * 512 + j * 128 + 128] = bk[h * 128:(h + 1) * 128]
            bkv[q * 512 + 256 + j * 128:q * 512 + 256 + j * 128 + 128] = bv[h * 128:(h + 1) * 128]
    wkvT = C(kv_cols.reshape(NT, 128, 4, 512).transpose(2, 1, 0, 3).reshape(512, NT * 512))
    qwp = qweight[f, :]
    qw_host = C(qwp.reshape(NT, 128, 8, 128, 3).transpose(3, 0, 4, 2, 1).reshape(128, NT * 3072))
    bq_t = C(bq.reshape(NT, 128).T)
    bo_t = C(bo[f].reshape(NT, 128).T)
    fsc_t = C(fscale[f].reshape(NT, 128).T)
    fbi_t = C(fbias[f].reshape(NT, 128).T)
    bkv_r = C(bkv.reshape(1, 2048))
    zp_r = C(fzp[f].reshape(1, D))

    nc = build_nc()
    shared = {
        "wqT": wqT, "woT": woT, "wkvT": wkvT, "qw": qw_host,
        "bq_t": bq_t, "bo_t": bo_t, "fsc_t": fsc_t, "fbi_t": fbi_t,
        "bkv_r": bkv_r, "zp_r": zp_r,
    }
    xTb = [C(x[b * BTOK:(b + 1) * BTOK, :][:, f].T) for b in range(2)]
    in_maps = []
    for c in range(NCORES):
        b, slot = c // 4, c % 4
        xc = np.roll(xTb[b], -slot * TOK, axis=1)
        m = {"xT": np.ascontiguousarray(xc)}
        m.update(shared)
        in_maps.append(m)

    res = run_bass_kernel_spmd(nc, in_maps, core_ids=list(range(NCORES)))

    finv = np.empty(D, np.int64)
    finv[f] = np.arange(D)
    outs = []
    for c in range(NCORES):
        oc = np.asarray(res.results[c]["out"])
        outs.append(oc.T[:, finv])
    full = np.concatenate(outs, axis=0).reshape(2, 8192, D).astype(np.float32)
    return full


if __name__ == "__main__":
    nc = build_nc()
    print("build ok, instructions:", nc.n_instructions())
